# version 03 (final)
# baseline (speedup 1.0000x reference)
import numpy as np
import concourse.bass as bass
import concourse.mybir as mybir
from concourse.alu_op_type import AluOpType as ALU
from concourse.bass_utils import run_bass_kernel_spmd
from contextlib import ExitStack

F32 = mybir.dt.float32
BF16 = mybir.dt.bfloat16
I32 = mybir.dt.int32
AF = mybir.ActivationFunctionType

LN2 = float(np.log(2.0))
IHL = float(0.5 / np.log(2.0))
EPS = 1e-5
NEGC_A = -8.0
NEGC_B = -8.0
S = 2048
D = 1024
NT = 8
SLOTB = [0, 4, 12, 24]


def build():
    nc = bass.Bass("TRN2", target_bir_lowering=False, debug=False,
                   detect_race_conditions=False)
    dri = lambda nm, shp: nc.dram_tensor(nm, shp, F32, kind="ExternalInput").ap()
    xT = dri("xT", [D, S])
    wqkvT = dri("wqkvT", [D, 3 * D])
    woutT = dri("woutT", [D, D])
    wfT = dri("wfT", [D, D])
    qkw = dri("qkw", [D, D])
    ovw = dri("ovw", [D, D])
    lng = dri("lng", [128, NT])
    lnb = dri("lnb", [128, NT])
    fbias = dri("fbias", [128, NT])
    boutw = dri("boutw", [128, NT])
    scl2 = dri("scl2", [2])
    ropeC = dri("ropeC", [128, S])
    ropeS = dri("ropeS", [128, S])
    maskd = dri("maskd", [128, S])
    identin = dri("identin", [128, 128])
    y = nc.dram_tensor("y", [D, S], F32, kind="ExternalOutput").ap()
    import os as _os
    DBG = bool(_os.environ.get("KDBG"))
    CUT = int(_os.environ.get("KCUT", "99"))
    KSC = _os.environ.get("KSC", "")
    if DBG:
        dbgA = nc.dram_tensor("dbgA", [D, S], F32, kind="ExternalOutput").ap()
        dbgB = nc.dram_tensor("dbgB", [D, S], F32, kind="ExternalOutput").ap()
        dbgH = nc.dram_tensor("dbgH", [D, S], F32, kind="ExternalOutput").ap()

    es = ExitStack()
    with es:
        def sbt(nm, p, n, dt=F32):
            return es.enter_context(nc.sbuf_tensor(nm, [p, n], dt))
        x_sb = sbt("x_sb", 128, NT * S)
        xhat = sbt("xhat", 128, NT * S, BF16)
        wstage = sbt("wstage", 128, 2048)
        wbf = sbt("wbf", 128, 4096, BF16)
        mu_bt = sbt("mu_bt", 128, S, BF16)
        a_bt = sbt("a_bt", 128, S, BF16)
        mu_scr = sbt("mu_scr", 128, S)
        rowb = sbt("rowb", 1, 1024, BF16)
        cst = sbt("cst", 128, 16)
        qcw = sbt("qcw", 128, 64)
        lngs = sbt("lngs", 128, NT)
        lnbs = sbt("lnbs", 128, NT)
        fbs = sbt("fbs", 128, NT)
        bos = sbt("bos", 128, NT)
        onesr = sbt("onesr", 1, 128, BF16)
        onescol = sbt("onescol", 128, 1, BF16)
        inv1024 = sbt("inv1024", 128, 1, BF16)
        identb = sbt("identb", 128, 128, BF16)
        zcols = sbt("zcols", 128, 8)
        zinvs = sbt("zinvs", 128, 8)
        init_es = ExitStack()
        onesrf = init_es.enter_context(nc.sbuf_tensor("onesrf", [1, 128], F32))
        srow = init_es.enter_context(nc.sbuf_tensor("srow", [1, 2], F32))
        izcal = init_es.enter_context(nc.sbuf_tensor("izcal", [128, 64], I32))
        r0 = mu_scr[0:1, 0:512]
        r1 = mu_scr[0:1, 512:1024]
        r2 = mu_scr[0:1, 1024:1536]

        pbk = [es.enter_context(nc.psum_tensor("pb%d" % i, [128, 512], F32))
               for i in range(7)]
        ptp = es.enter_context(nc.psum_tensor("ptp", [128, 1024], BF16))

        sems = {}
        tick = {}
        for nm in ("q", "g", "p", "a", "v", "w", "s"):
            sems[nm] = es.enter_context(nc.semaphore("sem_" + nm))
            tick[nm] = 0

        def inc(op, nm, k=1):
            op.then_inc(sems[nm], k)
            tick[nm] += k
            return tick[nm]

        bguard = {}

        def bwait(eng, b):
            if b in bguard:
                nm, t = bguard.pop(b)
                eng.wait_ge(sems[nm], t)

        wsguard = [("v", 0)]
        halfg = {0: ("p", 0), 1: ("p", 0)}

        def hwait(eng, h):
            nm, t = halfg[h]
            if t:
                eng.wait_ge(sems[nm], t)

        with nc.Block() as blk:
            @blk.sync
            def _(sy):
                for dt_ in range(0, 4):
                    inc(sy.dma_start(out=x_sb[:, dt_ * S:(dt_ + 1) * S],
                                     in_=xT[dt_ * 128:(dt_ + 1) * 128, :]), "q", 16)
                inc(sy.dma_start(out=srow[:, :], in_=scl2[:]), "q", 16)
                for t_, src in ((lngs, lng), (lnbs, lnb), (fbs, fbias), (bos, boutw)):
                    inc(sy.dma_start(out=t_[:, :], in_=src[:, :]), "q", 16)
                inc(sy.dma_start(out=wstage[:, 0:128], in_=identin[:, :]), "q", 16)

            @blk.gpsimd
            def _(gp):
                for dt_ in range(4, 8):
                    inc(gp.dma_start(out=x_sb[:, dt_ * S:(dt_ + 1) * S],
                                     in_=xT[dt_ * 128:(dt_ + 1) * 128, :]), "g", 16)

        ldq0, ldg0 = tick["q"], tick["g"]

        with nc.Block() as blk:
            @blk.vector
            def _(v):
                v.wait_ge(sems["q"], ldq0)
                v.wait_ge(sems["g"], ldg0)
                v.tensor_copy(out=identb[:, :], in_=wstage[:, 0:128])
                v.tensor_scalar(out=qcw[:, :], in0=wstage[:, 0:64], scalar1=0.0,
                                scalar2=1.5, op0=ALU.mult, op1=ALU.add)
                v.tensor_copy(out=izcal[:, :], in_=qcw[:, :])
                v.tensor_copy(out=qcw[:, :], in_=izcal[:, :])
                for col, s1, s2 in ((0, -0.5, 1024.5), (1, -0.5, 17.0),
                                    (2, 0.0, 9.0 * LN2), (3, 0.0, float(2.0 ** -24)),
                                    (4, 0.0, EPS), (5, 0.0, -55.0), (6, 0.0, -55.0)):
                    v.tensor_scalar(out=cst[:, col:col + 1], in0=qcw[:, 0:1],
                                    scalar1=s1, scalar2=s2, op0=ALU.mult, op1=ALU.add)
                v.tensor_scalar(out=onesr[:, :], in0=wstage[0:1, 0:128], scalar1=0.0,
                                scalar2=1.0, op0=ALU.mult, op1=ALU.add)
                v.tensor_scalar(out=onesrf[:, :], in0=wstage[0:1, 0:128], scalar1=0.0,
                                scalar2=1.0, op0=ALU.mult, op1=ALU.add)
                v.tensor_scalar(out=onescol[:, :], in0=wstage[:, 0:1], scalar1=0.0,
                                scalar2=1.0, op0=ALU.mult, op1=ALU.add)
                inc(v.tensor_scalar(out=inv1024[:, :], in0=wstage[:, 0:1], scalar1=0.0,
                                    scalar2=float(2.0 ** -10), op0=ALU.mult,
                                    op1=ALU.add), "v")
            wsguard[0] = ("v", tick["v"])

            @blk.tensor
            def _(te):
                te.wait_ge(sems["v"], tick["v"])
                te.matmul(pbk[6][:, 0:1], onesrf[0:1, :], srow[0:1, 0:1],
                          start=True, stop=True)
                inc(te.matmul(pbk[6][:, 1:2], onesrf[0:1, :], srow[0:1, 1:2],
                              start=True, stop=True), "p")

            @blk.scalar
            def _(sc):
                sc.wait_ge(sems["p"], tick["p"])
                inc(sc.activation(out=cst[:, 8:10], in_=pbk[6][:, 0:2],
                                  func=AF.Copy), "a")
            bguard[6] = ("a", tick["a"])

        with nc.Block() as blk:
            @blk.vector
            def _(v):
                v.wait_ge(sems["a"], tick["a"])
                t1 = inc(v.reciprocal(out=cst[:, 10:11], in_=cst[:, 8:9]), "v")
                v.wait_ge(sems["v"], t1)
                t2 = inc(v.tensor_tensor(out=cst[:, 12:13], in0=cst[:, 8:9],
                                         in1=cst[:, 9:10], op=ALU.mult), "v")
                v.wait_ge(sems["v"], t2)
                inc(v.tensor_scalar(out=cst[:, 12:13], in0=cst[:, 12:13],
                                    scalar1=float(2.0 ** -9), scalar2=None,
                                    op0=ALU.mult), "v")

            @blk.scalar
            def _(sc):
                sc.wait_ge(sems["v"], tick["v"])
                inc(sc.activation(out=cst[:, 11:12], in_=cst[:, 10:11],
                                  func=AF.Ln), "a")

        init_es.close()

        def emit_ln(site):
            with nc.sbuf_tensor("xbL%d" % tick["p"], [128, S], BF16) as xb, \
                 nc.sbuf_tensor("sqL%d" % tick["p"], [128, S], BF16) as sq:
                with nc.Block() as blk:
                    pv0 = tick["v"]
                    pp0 = tick["p"]

                    @blk.tensor
                    def _(te):
                        for c in range(4):
                            bwait(te, c)
                        for t_ in range(NT):
                            te.wait_ge(sems["v"], pv0 + t_ + 1)
                            for c in range(4):
                                te.matmul(pbk[c][0:1, :], inv1024[:, :],
                                          xb[:, c * 512:(c + 1) * 512],
                                          start=(t_ == 0), stop=(t_ == NT - 1))
                                op = te.matmul(pbk[c][64:65, :], inv1024[:, :],
                                               sq[:, c * 512:(c + 1) * 512],
                                               start=(t_ == 0), stop=(t_ == NT - 1),
                                               tile_position=(0, 64))
                                if c == 3:
                                    inc(op, "p")

                    @blk.vector
                    def _(v):
                        for t_ in range(NT):
                            if t_ >= 1:
                                v.wait_ge(sems["p"], pp0 + t_)
                            v.tensor_copy(out=xb[:, :],
                                          in_=x_sb[:, t_ * S:(t_ + 1) * S])
                            inc(v.tensor_tensor(out=sq[:, :], in0=xb[:, :],
                                                in1=xb[:, :], op=ALU.mult), "v")
                    assert tick["p"] == pp0 + NT
                    assert tick["v"] == pv0 + NT

                nA = 5 if site == 1 else 3
                nA2 = nA + 1
                nV = 4 if site == 1 else 2
                with nc.Block() as blk:
                    A0 = tick["a"]
                    V0 = tick["v"]
                    P0 = tick["p"]

                    @blk.tensor
                    def _(te):
                        bwait(te, 4)
                        bwait(te, 5)
                        for c in range(4):
                            te.wait_ge(sems["v"], V0 + c * nV + nV)
                            if c >= 1:
                                te.wait_ge(sems["a"], A0 + c * nA2)
                            te.matmul(pbk[4][:, :], onesr[0:1, :], rowb[0:1, 0:512],
                                      start=True, stop=True)
                            inc(te.matmul(pbk[5][:, :], onesr[0:1, :],
                                          rowb[0:1, 512:1024], start=True,
                                          stop=True), "p")

                    @blk.scalar
                    def _(sc):
                        sc.wait_ge(sems["p"], P0)
                        sc.wait_ge(sems["v"], V0)
                        for c in range(4):
                            if c >= 1:
                                sc.wait_ge(sems["v"], V0 + (c - 1) * nV + nV)
                            sc.activation(out=r0, in_=pbk[c][0:1, :], func=AF.Copy)
                            inc(sc.activation(out=r1, in_=pbk[c][64:65, :],
                                              func=AF.Copy), "a")
                            sc.wait_ge(sems["v"], V0 + c * nV + 1)
                            if site == 1:
                                inc(sc.activation(out=r0, in_=r1, func=AF.Ln,
                                                  bias=cst[0:1, 4:5]), "a")
                                inc(sc.activation(out=r2, in_=r0, func=AF.Exp,
                                                  scale=-1.0), "a")
                                sc.wait_ge(sems["v"], V0 + c * nV + 2)
                                inc(sc.activation(out=r2, in_=r2, func=AF.Ln,
                                                  bias=cst[0:1, 4:5]), "a")
                                sc.wait_ge(sems["v"], V0 + c * nV + 3)
                                inc(sc.activation(out=r1, in_=r0, func=AF.Exp,
                                                  scale=-0.5), "a")
                            else:
                                inc(sc.activation(out=r0, in_=r1, func=AF.Ln,
                                                  bias=cst[0:1, 4:5]), "a")
                                if site == 3:
                                    op = sc.activation(out=r1, in_=r0, func=AF.Exp,
                                                       scale=-0.5)
                                else:
                                    op = sc.activation(out=r1, in_=r0, func=AF.Exp,
                                                       scale=-0.5,
                                                       bias=cst[0:1, 11:12])
                                inc(op, "a")
                            sc.wait_ge(sems["p"], P0 + c + 1)
                            sc.activation(out=mu_bt[:, c * 512:(c + 1) * 512],
                                          in_=pbk[4][:, :], func=AF.Copy)
                            inc(sc.activation(out=a_bt[:, c * 512:(c + 1) * 512],
                                              in_=pbk[5][:, :], func=AF.Copy), "a")

                    @blk.vector
                    def _(v):
                        for c in range(4):
                            v.wait_ge(sems["a"], A0 + c * nA2 + 1)
                            if c >= 1:
                                v.wait_ge(sems["p"], P0 + c)
                            v.tensor_copy(out=rowb[:, 0:512], in_=r0)
                            v.tensor_tensor(out=r0, in0=r0, in1=r0, op=ALU.mult)
                            inc(v.tensor_tensor(out=r1, in0=r1, in1=r0,
                                                op=ALU.subtract), "v")
                            if site == 1:
                                v.wait_ge(sems["a"], A0 + c * nA2 + 3)
                                inc(v.tensor_tensor(out=r2, in0=r1, in1=r2,
                                                    op=ALU.mult), "v")
                                v.wait_ge(sems["a"], A0 + c * nA2 + 4)
                                inc(v.tensor_tensor(out=r0, in0=r0, in1=r2,
                                                    op=ALU.add), "v")
                                v.wait_ge(sems["a"], A0 + c * nA2 + 5)
                            else:
                                v.wait_ge(sems["a"], A0 + c * nA2 + 3)
                            inc(v.tensor_copy(out=rowb[:, 512:1024], in_=r1), "v")

                    for c in range(4):
                        bguard[c] = ("a", A0 + c * nA2 + 1)
                    bguard[4] = ("a", tick["a"])
                    bguard[5] = ("a", tick["a"])
                    assert tick["a"] == A0 + 4 * nA2
                    assert tick["v"] == V0 + 4 * nV


        def emit_ln_apply(site):
            with nc.Block() as blk:
                av = tick["a"]

                @blk.vector
                def _(v):
                    v.wait_ge(sems["a"], av)
                    for t_ in range(NT):
                        v.tensor_tensor(out=mu_scr[:, :],
                                        in0=x_sb[:, t_ * S:(t_ + 1) * S],
                                        in1=mu_bt[:, :], op=ALU.subtract)
                        if site == 1:
                            v.tensor_tensor(out=mu_scr[:, :], in0=mu_scr[:, :],
                                            in1=a_bt[:, :], op=ALU.mult)
                            op = v.tensor_scalar(out=xhat[:, t_ * S:(t_ + 1) * S],
                                                 in0=mu_scr[:, :],
                                                 scalar1=lngs[:, t_:t_ + 1],
                                                 scalar2=lnbs[:, t_:t_ + 1],
                                                 op0=ALU.mult, op1=ALU.add)
                        else:
                            op = v.tensor_tensor(out=xhat[:, t_ * S:(t_ + 1) * S],
                                                 in0=mu_scr[:, :], in1=a_bt[:, :],
                                                 op=ALU.mult)
                        inc(op, "v")

        with nc.sbuf_tensor("qksb", [128, 8 * S], BF16) as qksb, \
             nc.sbuf_tensor("vtok", [128, 16 * 520], BF16) as vtok:

            emit_ln(1)
            emit_ln_apply(1)
            xhat_ready = tick["v"]

            sc_prev_attnv_p = 0
            sc_prev_transp_p = 0

            for G in range(2 if CUT >= 4 else (1 if CUT >= 2 else 0)):
                gA = ExitStack()
                rsc = gA.enter_context(
                    nc.sbuf_tensor("rsc%d" % G, [128, 2 * S], BF16))
                tabs = gA.enter_context(
                    nc.sbuf_tensor("tabs%d" % G, [128, 2 * S], BF16))
                if True:

                    if True:
                        with nc.Block() as blk:
                            @blk.sync
                            def _(sy):
                                nm, t = wsguard[0]
                                sy.wait_ge(sems[nm], t)
                                inc(sy.dma_start(out=wstage[:, :],
                                                 in_=ropeC[:, :]), "q", 16)
                            qtC = tick["q"]

                            @blk.gpsimd
                            def _(gp):
                                gp.wait_ge(sems["v"], xhat_ready)
                                inc(gp.dma_start(out=mu_scr[:, :],
                                                 in_=ropeS[:, :]), "g", 16)
                            gtS = tick["g"]

                            @blk.vector
                            def _(v):
                                v.wait_ge(sems["q"], qtC)
                                inc(v.tensor_copy(out=tabs[:, 0:S],
                                                  in_=wstage[:, :]), "w")
                                v.wait_ge(sems["g"], gtS)
                                inc(v.tensor_copy(out=tabs[:, S:2 * S],
                                                  in_=mu_scr[:, :]), "v")
                            wsguard[0] = ("w", tick["w"])

                        rounds = []
                        for kind, base in (("q", 512 * G), ("k", D + 512 * G),
                                           ("v", 2 * D + 512 * G)):
                            for r_ in range(2):
                                rounds.append((kind, base + 256 * r_, r_))

                        rope_tick = {}
                        with nc.Block() as blk:
                            W0 = tick["w"]
                            P0 = tick["p"]
                            A0 = tick["a"]
                            V0 = tick["v"]
                            pbase = []
                            pc = P0
                            for (kind, e0, r_) in rounds:
                                n = 8 if kind != "v" else 16
                                pbase.append(pc)
                                pc += n
                            abase = []
                            ac = A0
                            for (kind, e0, r_) in rounds:
                                abase.append(ac)
                                if kind != "v":
                                    ac += 8
                            vbase = []
                            vc = V0
                            for (kind, e0, r_) in rounds:
                                vbase.append(vc)
                                vc += 2 if kind != "v" else 16

                            @blk.tensor
                            def _(te):
                                te.wait_ge(sems["v"], xhat_ready)
                                gq = 0
                                gv = 0
                                for i, (kind, e0, r_) in enumerate(rounds):
                                    te.wait_ge(sems["w"], W0 + i + 1)
                                    half = i % 2
                                    ws_ = wbf[:, half * 2048:(half + 1) * 2048]
                                    if kind != "v":
                                        for ms in range(2):
                                            for tc in range(4):
                                                b = gq % 4
                                                bwait(te, b)
                                                for dt_ in range(NT):
                                                    op = te.matmul(
                                                        pbk[b][:, :],
                                                        ws_[:, dt_ * 256 + ms * 128:
                                                            dt_ * 256 + ms * 128 + 128],
                                                        xhat[:, dt_ * S + tc * 512:
                                                             dt_ * S + tc * 512 + 512],
                                                        start=(dt_ == 0),
                                                        stop=(dt_ == NT - 1))
                                                inc(op, "p")
                                                bguard[b] = ("a",
                                                             abase[i] + ms * 4 + tc + 1)
                                                gq += 1
                                    else:
                                        for ts_ in range(16):
                                            b = 4 + gv % 2
                                            bwait(te, b)
                                            for dt_ in range(NT):
                                                op = te.matmul(
                                                    pbk[b][:, 0:256],
                                                    xhat[:, dt_ * S + ts_ * 128:
                                                         dt_ * S + ts_ * 128 + 128],
                                                    ws_[:, dt_ * 256:dt_ * 256 + 256],
                                                    start=(dt_ == 0),
                                                    stop=(dt_ == NT - 1))
                                            inc(op, "p")
                                            bguard[b] = ("v", vbase[i] + ts_ + 1)
                                            gv += 1
                            assert tick["p"] == pc

                            @blk.scalar
                            def _(sc):
                                gq = 0
                                for i, (kind, e0, r_) in enumerate(rounds):
                                    if kind == "v":
                                        continue
                                    koff = 0 if kind == "q" else 4 * S
                                    for ms in range(2):
                                        et = 2 * r_ + ms
                                        for tc in range(4):
                                            sc.wait_ge(sems["p"],
                                                       pbase[i] + ms * 4 + tc + 1)
                                            b = gq % 4
                                            inc(sc.activation(
                                                out=qksb[:, koff + et * S + tc * 512:
                                                         koff + et * S + tc * 512 + 512],
                                                in_=pbk[b][:, :], func=AF.Copy), "a")
                                            gq += 1
                            assert tick["a"] == ac

                            @blk.vector
                            def _(v):
                                for hg in range(8):
                                    vw = vtok[:, :].rearrange(
                                        "p (j h c) -> p j h c", j=16, h=8)
                                    v.tensor_scalar(out=vw[:, :, hg, 0],
                                                    in0=qcw[:, 0:16], scalar1=0.0,
                                                    scalar2=1.0, op0=ALU.mult,
                                                    op1=ALU.add)
                                gv = 0
                                for i, (kind, e0, r_) in enumerate(rounds):
                                    if kind != "v":
                                        for ms in range(2):
                                            et = 2 * r_ + ms
                                            koff = 0 if kind == "q" else 4 * S
                                            tile = qksb[:, koff + et * S:
                                                        koff + (et + 1) * S]
                                            v.wait_ge(sems["a"],
                                                      abase[i] + ms * 4 + 4)
                                            v.tensor_tensor(out=rsc[:, 0:S],
                                                            in0=tile,
                                                            in1=tabs[:, 0:S],
                                                            op=ALU.mult)
                                            for pp in range(4):
                                                p0 = pp * 32
                                                psw = (pp ^ 1) * 32
                                                v.tensor_scalar(
                                                    out=rsc[p0:p0 + 32, S:2 * S],
                                                    in0=tile[psw:psw + 32, :],
                                                    scalar1=1.0, scalar2=None,
                                                    op0=ALU.mult)
                                            v.tensor_tensor(out=rsc[:, S:2 * S],
                                                            in0=rsc[:, S:2 * S],
                                                            in1=tabs[:, S:2 * S],
                                                            op=ALU.mult)
                                            op = v.tensor_tensor(out=tile,
                                                                 in0=rsc[:, 0:S],
                                                                 in1=rsc[:, S:2 * S],
                                                                 op=ALU.add)
                                            t = inc(op, "v")
                                            rope_tick[(kind, et)] = t
                                            assert t == vbase[i] + ms + 1
                                    else:
                                        for ts_ in range(16):
                                            b = 4 + gv % 2
                                            v.wait_ge(sems["p"], pbase[i] + ts_ + 1)
                                            for hh in range(4):
                                                hg = 4 * r_ + hh
                                                op = v.tensor_scalar(
                                                    out=vtok[:, ts_ * 520 + hg * 65 + 1:
                                                             ts_ * 520 + hg * 65 + 65],
                                                    in0=pbk[b][:, hh * 64:hh * 64 + 64],
                                                    scalar1=1.0, scalar2=None,
                                                    op0=ALU.mult)
                                            t = inc(op, "v")
                                            assert t == vbase[i] + ts_ + 1
                                            gv += 1

                            wout_qt = []

                            @blk.sync
                            def _(sy):
                                for i, (kind, e0, r_) in enumerate(rounds):
                                    nm, t = wsguard[0]
                                    sy.wait_ge(sems[nm], t)
                                    for dt_ in range(NT):
                                        inc(sy.dma_start(
                                            out=wstage[:, dt_ * 256:(dt_ + 1) * 256],
                                            in_=wqkvT[dt_ * 128:(dt_ + 1) * 128,
                                                      e0:e0 + 256]), "q", 16)
                                    wsguard[0] = ("w", W0 + i + 1)
                                for r_ in range(2):
                                    nm, t = wsguard[0]
                                    sy.wait_ge(sems[nm], t)
                                    for ct in range(4):
                                        inc(sy.dma_start(
                                            out=wstage[:, ct * 512:(ct + 1) * 512],
                                            in_=woutT[512 * G + ct * 128:
                                                      512 * G + (ct + 1) * 128,
                                                      r_ * 512:(r_ + 1) * 512]),
                                            "q", 16)
                                    wout_qt.append(tick["q"])
                                    wsguard[0] = ("w", W0 + 6 + r_ + 1)

                            @blk.gpsimd
                            def _(gp):
                                QT = tick["q"] - 128
                                for i in range(6):
                                    gp.wait_ge(sems["q"], QT - (6 - 1 - i) * 128)
                                    half = i % 2
                                    if i >= 2:
                                        gp.wait_ge(sems["p"],
                                                   pbase[i - 2] +
                                                   (8 if rounds[i - 2][0] != "v"
                                                    else 16))
                                    else:
                                        hwait(gp, half)
                                    inc(gp.tensor_copy(
                                        out=wbf[:, half * 2048:(half + 1) * 2048],
                                        in_=wstage[:, :]), "w")
                                for r_ in range(2):
                                    gp.wait_ge(sems["q"], wout_qt[r_])
                                    gp.wait_ge(sems["p"],
                                               pbase[4 + r_] + 16)
                                    inc(gp.tensor_copy(
                                        out=wbf[:, r_ * 2048:(r_ + 1) * 2048],
                                        in_=wstage[:, :]), "w")
                            assert tick["w"] == W0 + 8
                            wout_w = tick["w"]

                    gA.close()
                    gB = ExitStack()
                    odmaj = gB.enter_context(
                        nc.sbuf_tensor("odmaj%d" % G, [128, 4 * 512], BF16))
                    otok = gB.enter_context(
                        nc.sbuf_tensor("otok%d" % G, [128, 4 * 512], BF16))
                    pPa = gB.enter_context(
                        nc.sbuf_tensor("pPa%d" % G, [128, 16 * 512], BF16))
                    zsb = gB.enter_context(
                        nc.sbuf_tensor("zsb%d" % G, [128, 512], BF16))
                    qkv_done_p = tick["p"]
                    vtok_done_v = tick["v"]

                    for ic in range(4 if CUT >= 3 else 0):
                        with nc.Block() as blk:
                            P0 = tick["p"]
                            A0 = tick["a"]
                            V0 = tick["v"]
                            zrow = mu_scr[0:1, 1024:1536]
                            zirow = mu_scr[0:1, 1536:2048]

                            @blk.tensor
                            def _(te):
                                te.wait_ge(sems["v"], vtok_done_v)
                                for hg in range(8):
                                    et = hg // 2
                                    p0 = 64 * (hg % 2)
                                    for jb in range(16):
                                        b = jb % 2
                                        bwait(te, b)
                                        op = te.matmul(
                                            pbk[b][:, :],
                                            qksb[p0:p0 + 64,
                                                 (4 + et) * S + jb * 128:
                                                 (4 + et) * S + jb * 128 + 128],
                                            qksb[p0:p0 + 64,
                                                 et * S + ic * 512:
                                                 et * S + ic * 512 + 512],
                                            start=True, stop=True,
                                            tile_position=(p0, 0))
                                        inc(op, "p")
                                        bguard[b] = ("a", A0 + hg * 23 + jb + 1)
                                    te.wait_ge(sems["a"], A0 + hg * 23 + 16)
                                    bwait(te, 4)
                                    for jt in range(16):
                                        op = te.matmul(
                                            pbk[4][0:1, :], onescol[:, :],
                                            pPa[:, jt * 512:jt * 512 + 512],
                                            start=(jt == 0), stop=(jt == 15))
                                    inc(op, "p")
                                    te.wait_ge(sems["a"], A0 + hg * 23 + 17)
                                    te.wait_ge(sems["v"], V0 + hg * 10 + 1)
                                    inc(te.matmul(pbk[4][:, :], onesr[0:1, :],
                                                  rowb[0:1, 0:512], start=True,
                                                  stop=True), "p")
                                    bguard[4] = ("a", A0 + hg * 23 + 19)
                                    for ib in range(4):
                                        b = 2 + (hg * 4 + ib) % 2
                                        bwait(te, b)
                                        te.wait_ge(sems["v"], V0 + hg * 10 + 2)
                                        for jt in range(16):
                                            op = te.matmul(
                                                pbk[b][:, 0:65],
                                                pPa[:, jt * 512 + ib * 128:
                                                    jt * 512 + ib * 128 + 128],
                                                vtok[:, jt * 520 + hg * 65:
                                                     jt * 520 + hg * 65 + 65],
                                                start=(jt == 0), stop=(jt == 15))
                                        inc(op, "p")
                                        bguard[b] = ("v",
                                                     V0 + hg * 10 + 2 * ib + 4)
                                for ibg in range(4):
                                    half = 0
                                    bwait(te, ("pt", half))
                                    te.wait_ge(sems["v"],
                                               V0 + 7 * 10 + 2 * ibg + 4)
                                    for ct in range(4):
                                        op = te.transpose(
                                            ptp[:, half * 512 + ct * 128:
                                                half * 512 + ct * 128 + 128],
                                            otok[:, ibg * 512 + ct * 128:
                                                 ibg * 512 + ct * 128 + 128],
                                            identb[:, :])
                                    inc(op, "p")
                                    bguard[("pt", half)] = ("v",
                                                            V0 + 80 + ibg + 1)
                                te.wait_ge(sems["w"], wout_w)
                                te.wait_ge(sems["v"], V0 + 84)
                                for esub in range(8):
                                    b = 5 + esub % 2
                                    bwait(te, b)
                                    for ct in range(4):
                                        op = te.matmul(
                                            pbk[b][:, :],
                                            wbf[:, (esub // 4) * 2048 +
                                                ct * 512 + (esub % 4) * 128:
                                                (esub // 4) * 2048 + ct * 512 +
                                                (esub % 4) * 128 + 128],
                                            odmaj[:, ct * 512:ct * 512 + 512],
                                            start=(ct == 0), stop=(ct == 3))
                                    inc(op, "p")
                                    bguard[b] = ("v", V0 + 84 + esub + 1)
                            assert tick["p"] == P0 + 188

                            @blk.scalar
                            def _(sc):
                                for hg in range(8):
                                    if hg == 0:
                                        if sc_prev_attnv_p:
                                            sc.wait_ge(sems["p"], sc_prev_attnv_p)
                                    else:
                                        sc.wait_ge(sems["p"],
                                                   P0 + (hg - 1) * 22 + 22)
                                    for jb in range(16):
                                        sc.wait_ge(sems["p"],
                                                   P0 + hg * 22 + jb + 1)
                                        b = jb % 2
                                        inc(sc.activation(
                                            out=pPa[:, jb * 512:jb * 512 + 512],
                                            in_=pbk[b][:, :], func=AF.Exp,
                                            scale=0.0625,
                                            bias=cst[:, 5:6]), "a")
                                    sc.wait_ge(sems["p"], P0 + hg * 22 + 17)
                                    inc(sc.activation(out=zrow,
                                                      in_=pbk[4][0:1, :],
                                                      func=AF.Copy), "a")
                                    sc.activation(out=zirow, in_=zrow,
                                                  func=AF.Ln)
                                    inc(sc.activation(out=zirow, in_=zirow,
                                                      func=AF.Exp,
                                                      scale=-1.0), "a")
                                    sc.wait_ge(sems["p"], P0 + hg * 22 + 18)
                                    inc(sc.activation(out=zsb[:, :],
                                                      in_=pbk[4][:, :],
                                                      func=AF.Copy), "a")
                                    for ib in range(4):
                                        sc.wait_ge(sems["p"],
                                                   P0 + hg * 22 + 18 + ib + 1)
                                        b = 2 + (hg * 4 + ib) % 2
                                        zi = (hg * 4 + ib) % 8
                                        inc(sc.activation(
                                            out=zcols[:, zi:zi + 1],
                                            in_=pbk[b][:, 0:1],
                                            func=AF.Copy), "a")
                            assert tick["a"] == A0 + 184

                            @blk.vector
                            def _(v):
                                if sc_prev_transp_p:
                                    v.wait_ge(sems["p"], sc_prev_transp_p)
                                for hg in range(8):
                                    v.wait_ge(sems["a"], A0 + hg * 23 + 18)
                                    if hg >= 1:
                                        v.wait_ge(sems["p"],
                                                  P0 + (hg - 1) * 22 + 18)
                                    inc(v.tensor_copy(out=rowb[:, 0:512],
                                                      in_=zirow), "v")
                                    v.wait_ge(sems["a"], A0 + hg * 23 + 19)
                                    for jt in range(16):
                                        sl = pPa[:, jt * 512:jt * 512 + 512]
                                        v.tensor_tensor(out=sl, in0=sl,
                                                        in1=zsb[:, :],
                                                        op=ALU.mult)
                                        op = v.tensor_tensor(out=sl, in0=sl,
                                                             in1=sl,
                                                             op=ALU.mult)
                                    inc(op, "v")
                                    for ib in range(4):
                                        v.wait_ge(sems["a"],
                                                  A0 + hg * 23 + 19 + ib + 1)
                                        zi = (hg * 4 + ib) % 8
                                        b = 2 + (hg * 4 + ib) % 2
                                        t = inc(v.reciprocal(
                                            out=zinvs[:, zi:zi + 1],
                                            in_=zcols[:, zi:zi + 1]), "v")
                                        v.wait_ge(sems["v"], t)
                                        inc(v.tensor_scalar(
                                            out=otok[:, ib * 512 + hg * 64:
                                                     ib * 512 + hg * 64 + 64],
                                            in0=pbk[b][:, 1:65],
                                            scalar1=zinvs[:, zi:zi + 1],
                                            scalar2=None, op0=ALU.mult), "v")
                                for ibg in range(4):
                                    v.wait_ge(sems["p"], P0 + 176 + ibg + 1)
                                    for ct in range(4):
                                        op = v.tensor_scalar(
                                            out=odmaj[:, ct * 512 + ibg * 128:
                                                     ct * 512 + ibg * 128 + 128],
                                            in0=ptp[:, ct * 128:
                                                    ct * 128 + 128],
                                            scalar1=1.0, scalar2=None,
                                            op0=ALU.mult)
                                    inc(op, "v")
                                for esub in range(8):
                                    v.wait_ge(sems["p"], P0 + 180 + esub + 1)
                                    b = 5 + esub % 2
                                    xs = x_sb[:, esub * S + ic * 512:
                                              esub * S + ic * 512 + 512]
                                    if G == 0:
                                        inc(v.tensor_tensor(out=xs, in0=xs,
                                                            in1=pbk[b][:, :],
                                                            op=ALU.add), "v")
                                    else:
                                        v.tensor_scalar(
                                            out=mu_scr[:, 0:512],
                                            in0=pbk[b][:, :],
                                            scalar1=bos[:, esub:esub + 1],
                                            scalar2=None, op0=ALU.add)
                                        inc(v.tensor_tensor(
                                            out=xs, in0=xs,
                                            in1=mu_scr[:, 0:512],
                                            op=ALU.add), "v")
                            assert tick["v"] == V0 + 92

                            sc_prev_attnv_p = P0 + 176
                            sc_prev_transp_p = P0 + 180
                    halfg[0] = ("p", tick["p"])
                    halfg[1] = ("p", tick["p"])
                    gB.close()

        def emit_dump(dst, src_sb):
            with nc.Block() as blk:
                fv = tick["v"]
                fa = tick["a"]

                @blk.gpsimd
                def _(gp):
                    gp.wait_ge(sems["v"], fv)
                    gp.wait_ge(sems["a"], fa)
                    for dt_ in range(NT):
                        inc(gp.dma_start(out=dst[dt_ * 128:(dt_ + 1) * 128, :],
                                         in_=src_sb[:, dt_ * S:(dt_ + 1) * S]),
                            "g", 16)
                    gp.wait_ge(sems["g"], tick["g"])

        def emit_ffn(site):
            emit_ln(site)
            if DBG and site == 2:
                with nc.Block() as blk:
                    fa = tick["a"]

                    @blk.vector
                    def _(v):
                        v.wait_ge(sems["a"], fa)
                        inc(v.tensor_copy(out=mu_scr[:, :], in_=mu_bt[:, :]), "v")

                    @blk.gpsimd
                    def _(gp):
                        gp.wait_ge(sems["v"], tick["v"])
                        inc(gp.dma_start(out=dbgH[0:128, :], in_=mu_scr[:, :]),
                            "g", 16)
                        gp.wait_ge(sems["g"], tick["g"])
                with nc.Block() as blk:
                    fg = tick["g"]

                    @blk.vector
                    def _(v):
                        v.wait_ge(sems["g"], fg)
                        inc(v.tensor_copy(out=mu_scr[:, :], in_=a_bt[:, :]), "v")

                    @blk.gpsimd
                    def _(gp):
                        gp.wait_ge(sems["v"], tick["v"])
                        inc(gp.dma_start(out=dbgH[128:256, :], in_=mu_scr[:, :]),
                            "g", 16)
                        gp.wait_ge(sems["g"], tick["g"])
            with nc.sbuf_tensor("F2_%d" % site, [128, S], F32) as F2, \
                 nc.sbuf_tensor("F3_%d" % site, [128, S], F32) as F3, \
                 nc.sbuf_tensor("F4_%d" % site, [128, S], F32) as F4, \
                 nc.sbuf_tensor("izq_%d" % site, [128, S], I32) as izq:
                with nc.Block() as blk:
                    A0 = tick["a"]
                    V0 = tick["v"]
                    W0 = tick["w"]
                    AVLN = tick["a"]

                    @blk.vector
                    def _(v):
                        v.wait_ge(sems["a"], AVLN)
                        for t_ in range(NT):
                            xs = x_sb[:, t_ * S:(t_ + 1) * S]
                            v.tensor_tensor(out=mu_scr[:, :], in0=xs,
                                            in1=mu_bt[:, :], op=ALU.subtract)
                            v.tensor_tensor(out=mu_scr[:, :], in0=mu_scr[:, :],
                                            in1=a_bt[:, :], op=ALU.mult)
                            v.tensor_scalar(out=mu_scr[:, :], in0=mu_scr[:, :],
                                            scalar1=448.0, scalar2=-448.0,
                                            op0=ALU.min, op1=ALU.max)
                            inc(v.tensor_tensor(out=F2[:, :], in0=mu_scr[:, :],
                                                in1=mu_scr[:, :], op=ALU.mult), "v")
                            v.wait_ge(sems["a"], A0 + t_ * 2 + 1)
                            v.tensor_scalar(out=F2[:, :], in0=F3[:, :],
                                            scalar1=IHL, scalar2=cst[:, 0:1],
                                            op0=ALU.mult, op1=ALU.add)
                            v.tensor_copy(out=izq[:, :], in_=F2[:, :])
                            v.tensor_copy(out=F2[:, :], in_=izq[:, :])
                            inc(v.tensor_scalar(out=F2[:, :], in0=F2[:, :],
                                                scalar1=1018.0, scalar2=1018.0,
                                                op0=ALU.max, op1=ALU.subtract), "v")
                            v.wait_ge(sems["a"], A0 + t_ * 2 + 2)
                            v.tensor_tensor(out=F2[:, :], in0=mu_scr[:, :],
                                            in1=F4[:, :], op=ALU.mult)
                            v.tensor_scalar(out=F2[:, :], in0=F2[:, :],
                                            scalar1=cst[:, 1:2], scalar2=None,
                                            op0=ALU.add)
                            v.tensor_copy(out=izq[:, :], in_=F2[:, :])
                            v.tensor_copy(out=F2[:, :], in_=izq[:, :])
                            inc(v.scalar_tensor_tensor(
                                out=xhat[:, t_ * S:(t_ + 1) * S], in0=F2[:, :],
                                scalar=16.0, in1=F3[:, :], op0=ALU.subtract,
                                op1=ALU.mult), "v")

                    @blk.scalar
                    def _(sc):
                        for t_ in range(NT):
                            sc.wait_ge(sems["v"], V0 + t_ * 3 + 1)
                            inc(sc.activation(out=F3[:, :], in_=F2[:, :],
                                              func=AF.Ln, bias=cst[:, 3:4]), "a")
                            sc.wait_ge(sems["v"], V0 + t_ * 3 + 2)
                            sc.activation(out=F3[:, :], in_=F2[:, :],
                                          func=AF.Exp, scale=LN2)
                            inc(sc.activation(out=F4[:, :], in_=F2[:, :],
                                              func=AF.Exp, scale=-LN2,
                                              bias=cst[:, 2:3]), "a")
                    assert tick["v"] == V0 + NT * 3
                    assert tick["a"] == A0 + NT * 2

                    @blk.sync
                    def _(sy):
                        for j in range(2):
                            nm, t = wsguard[0]
                            sy.wait_ge(sems[nm], t)
                            for dt_ in range(NT):
                                inc(sy.dma_start(
                                    out=wstage[:, dt_ * 256:(dt_ + 1) * 256],
                                    in_=wfT[dt_ * 128:(dt_ + 1) * 128,
                                            256 * j:256 * j + 256]), "q", 16)
                            wsguard[0] = ("w", W0 + j + 1)
                            globals()["_wf_qt_%d_%d" % (site, j)] = tick["q"]

                    @blk.gpsimd
                    def _(gp):
                        for j in range(2):
                            gp.wait_ge(sems["q"],
                                       globals()["_wf_qt_%d_%d" % (site, j)])
                            hwait(gp, j % 2)
                            inc(gp.tensor_copy(
                                out=wbf[:, (j % 2) * 2048:(j % 2 + 1) * 2048],
                                in_=wstage[:, :]), "w")
                    assert tick["w"] == W0 + 2

                with nc.Block() as blk:
                    P0 = tick["p"]
                    V0 = tick["v"]
                    W0 = tick["w"]
                    y8done = tick["v"]
                    mm_round_p = {}

                    @blk.tensor
                    def _(te):
                        te.wait_ge(sems["v"], y8done)
                        idx = 0
                        for esub in range(8):
                            j = esub // 2
                            te.wait_ge(sems["w"], W0 - 2 + j + 1)
                            for icc in range(4):
                                b = idx % 4
                                bwait(te, b)
                                for dt_ in range(NT):
                                    op = te.matmul(
                                        pbk[b][:, :],
                                        wbf[:, (j % 2) * 2048 + dt_ * 256 +
                                            (esub % 2) * 128:
                                            (j % 2) * 2048 + dt_ * 256 +
                                            (esub % 2) * 128 + 128],
                                        xhat[:, dt_ * S + icc * 512:
                                             dt_ * S + icc * 512 + 512],
                                        start=(dt_ == 0), stop=(dt_ == NT - 1))
                                inc(op, "p")
                                bguard[b] = ("v", V0 + idx + 1)
                                idx += 1
                            mm_round_p[j] = tick["p"]
                    halfg[0] = ("p", mm_round_p[2])
                    halfg[1] = ("p", mm_round_p[3])

                    @blk.sync
                    def _(sy):
                        for j in (2, 3):
                            nm, t = wsguard[0]
                            sy.wait_ge(sems[nm], t)
                            for dt_ in range(NT):
                                inc(sy.dma_start(
                                    out=wstage[:, dt_ * 256:(dt_ + 1) * 256],
                                    in_=wfT[dt_ * 128:(dt_ + 1) * 128,
                                            256 * j:256 * j + 256]), "q", 16)
                            wsguard[0] = ("w", W0 + (j - 2) + 1)
                            globals()["_wf_qt_%d_%d" % (site, j)] = tick["q"]

                    @blk.gpsimd
                    def _(gp):
                        for j in (2, 3):
                            gp.wait_ge(sems["q"],
                                       globals()["_wf_qt_%d_%d" % (site, j)])
                            gp.wait_ge(sems["p"], mm_round_p[j - 2])
                            inc(gp.tensor_copy(
                                out=wbf[:, (j % 2) * 2048:(j % 2 + 1) * 2048],
                                in_=wstage[:, :]), "w")

                    @blk.vector
                    def _(v):
                        idx = 0
                        for esub in range(8):
                            for icc in range(4):
                                v.wait_ge(sems["p"], P0 + idx + 1)
                                b = idx % 4
                                v.tensor_scalar(out=F2[:, 0:512],
                                                in0=pbk[b][:, :],
                                                scalar1=cst[:, 12:13],
                                                scalar2=fbs[:, esub:esub + 1],
                                                op0=ALU.mult, op1=ALU.add)
                                xs = x_sb[:, esub * S + icc * 512:
                                          esub * S + icc * 512 + 512]
                                inc(v.tensor_tensor(out=xs, in0=xs,
                                                    in1=F2[:, 0:512],
                                                    op=ALU.add), "v")
                                idx += 1
                    assert tick["v"] == V0 + 32


        if DBG:
            emit_dump(dbgA, x_sb)
        if CUT >= 5:
            emit_ffn(2)
        if DBG:
            emit_dump(dbgB, x_sb)
        if CUT < 8:
            pass

        with nc.sbuf_tensor("maskb", [128, S], BF16) as maskb, \
             nc.sbuf_tensor("Qic", [128, 8 * 512], BF16) as Qic, \
             nc.sbuf_tensor("pPb", [128, 40 * 512], BF16) as pPb, \
             nc.sbuf_tensor("ctxb", [128, 8 * 512], BF16) as ctxb, \
             nc.sbuf_tensor("zbsb", [128, 512], BF16) as zbsb:

            if CUT >= 6:
                emit_ln(3)
                emit_ln_apply(3)
            xhat3_v = tick["v"]

            if CUT >= 6:
              with nc.Block() as blk:
                @blk.sync
                def _(sy):
                    nm, t = wsguard[0]
                    sy.wait_ge(sems[nm], t)
                    inc(sy.dma_start(out=wstage[:, :], in_=maskd[:, :]), "q", 16)
                mqt = tick["q"]

                @blk.gpsimd
                def _(gp):
                    gp.wait_ge(sems["q"], mqt)
                    inc(gp.tensor_copy(out=maskb[:, :], in_=wstage[:, :]), "w")
                wsguard[0] = ("w", tick["w"])

            diag_mask_v = {}
            for ic in range(4 if CUT >= 6 else 0):
                njb = 4 * ic + 4
                with nc.Block() as blk:
                    P0 = tick["p"]
                    A0 = tick["a"]
                    V0 = tick["v"]
                    W0 = tick["w"]
                    QT0 = tick["q"]
                    qk_mm_p = {}

                    @blk.tensor
                    def _(te):
                        te.wait_ge(sems["v"], xhat3_v)
                        for j in range(4):
                            te.wait_ge(sems["w"], W0 + j + 1)
                            for esub in (2 * j, 2 * j + 1):
                                b = 4 + esub % 2
                                bwait(te, b)
                                for dt_ in range(NT):
                                    op = te.matmul(
                                        pbk[b][:, :],
                                        wbf[:, (j % 2) * 2048 + dt_ * 256 +
                                            (esub % 2) * 128:
                                            (j % 2) * 2048 + dt_ * 256 +
                                            (esub % 2) * 128 + 128],
                                        xhat[:, dt_ * S + ic * 512:
                                             dt_ * S + ic * 512 + 512],
                                        start=(dt_ == 0), stop=(dt_ == NT - 1))
                                inc(op, "p")
                                bguard[b] = ("a", A0 + esub + 1)
                            qk_mm_p[j] = tick["p"]
                        te.wait_ge(sems["a"], A0 + 8)
                        for jb in range(njb):
                            b = jb % 2
                            bwait(te, b)
                            for et in range(NT):
                                op = te.matmul(
                                    pbk[b][:, :],
                                    xhat[:, et * S + jb * 128:
                                         et * S + jb * 128 + 128],
                                    Qic[:, et * 512:et * 512 + 512],
                                    start=(et == 0), stop=(et == NT - 1))
                            inc(op, "p")
                            bguard[b] = ("a", A0 + 8 + jb + 1)
                    assert tick["p"] == P0 + 8 + njb

                    @blk.scalar
                    def _(sc):
                        for esub in range(8):
                            sc.wait_ge(sems["p"], P0 + esub + 1)
                            b = 4 + esub % 2
                            inc(sc.activation(out=Qic[:, esub * 512:esub * 512 + 512],
                                              in_=pbk[b][:, :], func=AF.Copy), "a")
                        for jb in range(njb):
                            sc.wait_ge(sems["p"], P0 + 8 + jb + 1)
                            b = jb % 2
                            inc(sc.activation(
                                out=pPb[:, (SLOTB[ic] + jb) * 512:
                                        (SLOTB[ic] + jb) * 512 + 512],
                                in_=pbk[b][:, :], func=AF.Exp,
                                scale=float(1.0 / 64.0), bias=cst[:, 6:7]), "a")
                    assert tick["a"] == A0 + 8 + njb

                    @blk.vector
                    def _(v):
                        for jb in range(4 * ic, 4 * ic + 4):
                            v.wait_ge(sems["a"], A0 + 8 + jb + 1)
                            rr = jb - 4 * ic
                            sl = pPb[:, (SLOTB[ic] + jb) * 512:
                                     (SLOTB[ic] + jb) * 512 + 512]
                            inc(v.tensor_tensor(out=sl, in0=sl,
                                                in1=maskb[:, rr * 512:rr * 512 + 512],
                                                op=ALU.mult), "v")
                        diag_mask_v[ic] = tick["v"]

                    @blk.sync
                    def _(sy):
                        for j in range(4):
                            nm, t = wsguard[0]
                            sy.wait_ge(sems[nm], t)
                            for dt_ in range(NT):
                                inc(sy.dma_start(
                                    out=wstage[:, dt_ * 256:(dt_ + 1) * 256],
                                    in_=qkw[dt_ * 128:(dt_ + 1) * 128,
                                            256 * j:256 * j + 256]), "q", 16)
                            wsguard[0] = ("w", W0 + j + 1)
                            globals()["_qk_qt_%d_%d" % (ic, j)] = tick["q"]

                    @blk.gpsimd
                    def _(gp):
                        for j in range(4):
                            gp.wait_ge(sems["q"],
                                       globals()["_qk_qt_%d_%d" % (ic, j)])
                            if j >= 2:
                                gp.wait_ge(sems["p"], qk_mm_p[j - 2])
                            else:
                                hwait(gp, j % 2)
                            inc(gp.tensor_copy(
                                out=wbf[:, (j % 2) * 2048:(j % 2 + 1) * 2048],
                                in_=wstage[:, :]), "w")
                    assert tick["w"] == W0 + 4
                    halfg[0] = ("p", qk_mm_p[2])
                    halfg[1] = ("p", qk_mm_p[3])


            if CUT >= 7:
              with nc.Block() as blk:
                P0 = tick["p"]
                V0 = tick["v"]

                @blk.tensor
                def _(te):
                    idx = 0
                    for jt in range(16):
                        for dtg in range(2):
                            half = 0
                            bwait(te, ("pt", half))
                            for k in range(4):
                                op = te.transpose(
                                    ptp[:, half * 512 + k * 128:
                                        half * 512 + k * 128 + 128],
                                    xhat[:, (dtg * 4 + k) * S + jt * 128:
                                         (dtg * 4 + k) * S + jt * 128 + 128],
                                    identb[:, :])
                            inc(op, "p")
                            bguard[("pt", half)] = ("v", V0 + idx + 1)
                            idx += 1

                @blk.vector
                def _(v):
                    idx = 0
                    for jt in range(16):
                        for dtg in range(2):
                            half = 0
                            v.wait_ge(sems["p"], P0 + idx + 1)
                            for k in range(4):
                                op = v.tensor_scalar(
                                    out=xhat[:, (dtg * 4 + k) * S + jt * 128:
                                             (dtg * 4 + k) * S + jt * 128 + 128],
                                    in0=ptp[:, half * 512 + k * 128:
                                            half * 512 + k * 128 + 128],
                                    scalar1=1.0, scalar2=None, op0=ALU.mult)
                            inc(op, "v")
                            idx += 1
                assert tick["v"] == V0 + 32
            xtok_v = tick["v"]

            for ic in range(4 if CUT >= 7 else 0):
                njt = 4 * ic + 4
                with nc.Block() as blk:
                    P0 = tick["p"]
                    A0 = tick["a"]
                    V0 = tick["v"]
                    W0 = tick["w"]
                    ctx_mm_p = {}
                    zrow = mu_scr[0:1, 1024:1536]
                    zirow = mu_scr[0:1, 1536:2048]

                    def slot(jt):
                        return pPb[:, (SLOTB[ic] + jt) * 512:
                                   (SLOTB[ic] + jt) * 512 + 512]

                    @blk.tensor
                    def _(te):
                        te.wait_ge(sems["v"], xtok_v)
                        te.wait_ge(sems["v"], diag_mask_v[ic])
                        bwait(te, 0)
                        for jt in range(njt):
                            op = te.matmul(pbk[0][0:1, :], onescol[:, :],
                                           slot(jt), start=(jt == 0),
                                           stop=(jt == njt - 1))
                        inc(op, "p")
                        te.wait_ge(sems["a"], A0 + 1)
                        te.wait_ge(sems["v"], V0 + 1)
                        inc(te.matmul(pbk[0][:, :], onesr[0:1, :],
                                      rowb[0:1, 0:512], start=True,
                                      stop=True), "p")
                        bguard[0] = ("a", A0 + 2)
                        te.wait_ge(sems["v"], V0 + 2)
                        bwait(te, 1)
                        for jt in range(njt):
                            op = te.matmul(pbk[1][0:1, :], onescol[:, :],
                                           slot(jt), start=(jt == 0),
                                           stop=(jt == njt - 1))
                        inc(op, "p")
                        te.wait_ge(sems["a"], A0 + 3)
                        te.wait_ge(sems["v"], V0 + 3)
                        inc(te.matmul(pbk[1][:, :], onesr[0:1, :],
                                      rowb[0:1, 512:1024], start=True,
                                      stop=True), "p")
                        bguard[1] = ("a", A0 + 4)
                        for dsub in range(8):
                            b = 2 + dsub % 2
                            bwait(te, b)
                            for jt in range(njt):
                                op = te.matmul(
                                    pbk[b][:, :],
                                    xhat[:, dsub * S + jt * 128:
                                         dsub * S + jt * 128 + 128],
                                    slot(jt), start=(jt == 0),
                                    stop=(jt == njt - 1))
                            inc(op, "p")
                            bguard[b] = ("v", V0 + 3 + dsub + 1)
                        te.wait_ge(sems["v"], V0 + 11)
                        for ei, esub in enumerate(range(8)):
                            j = esub // 2
                            te.wait_ge(sems["w"], W0 + j + 1)
                            b = 4 + ei % 2
                            bwait(te, b)
                            for dsub in range(NT):
                                op = te.matmul(
                                    pbk[b][:, :],
                                    wbf[:, (j % 2) * 2048 + dsub * 256 +
                                        (esub % 2) * 128:
                                        (j % 2) * 2048 + dsub * 256 +
                                        (esub % 2) * 128 + 128],
                                    ctxb[:, dsub * 512:dsub * 512 + 512],
                                    start=(dsub == 0), stop=(dsub == NT - 1))
                            inc(op, "p")
                            bguard[b] = ("v", V0 + 11 + ei + 1)
                            if esub % 2 == 1:
                                ctx_mm_p[j] = tick["p"]
                    assert tick["p"] == P0 + 20

                    @blk.scalar
                    def _(sc):
                        sc.wait_ge(sems["p"], P0 + 1)
                        sc.activation(out=zrow, in_=pbk[0][0:1, :], func=AF.Copy)
                        sc.activation(out=zirow, in_=zrow, func=AF.Ln)
                        inc(sc.activation(out=zirow, in_=zirow, func=AF.Exp,
                                          scale=-1.0), "a")
                        sc.wait_ge(sems["p"], P0 + 2)
                        inc(sc.activation(out=zbsb[:, :], in_=pbk[0][:, :],
                                          func=AF.Copy), "a")
                        sc.wait_ge(sems["p"], P0 + 3)
                        sc.activation(out=zrow, in_=pbk[1][0:1, :], func=AF.Copy)
                        sc.activation(out=zirow, in_=zrow, func=AF.Ln)
                        inc(sc.activation(out=zirow, in_=zirow, func=AF.Exp,
                                          scale=-1.0), "a")
                        sc.wait_ge(sems["p"], P0 + 4)
                        inc(sc.activation(out=zbsb[:, :], in_=pbk[1][:, :],
                                          func=AF.Copy), "a")
                    assert tick["a"] == A0 + 4

                    @blk.vector
                    def _(v):
                        v.wait_ge(sems["a"], A0 + 1)
                        inc(v.tensor_copy(out=rowb[:, 0:512], in_=zirow), "v")
                        v.wait_ge(sems["a"], A0 + 2)
                        for jt in range(njt):
                            sl = slot(jt)
                            v.tensor_tensor(out=sl, in0=sl, in1=zbsb[:, :],
                                            op=ALU.mult)
                            op = v.tensor_tensor(out=sl, in0=sl, in1=sl,
                                                 op=ALU.mult)
                        inc(op, "v")
                        v.wait_ge(sems["a"], A0 + 3)
                        inc(v.tensor_copy(out=rowb[:, 512:1024], in_=zirow), "v")
                        for dsub in range(8):
                            v.wait_ge(sems["p"], P0 + 4 + dsub + 1)
                            v.wait_ge(sems["a"], A0 + 4)
                            b = 2 + dsub % 2
                            inc(v.tensor_tensor(
                                out=ctxb[:, dsub * 512:dsub * 512 + 512],
                                in0=pbk[b][:, :], in1=zbsb[:, :],
                                op=ALU.mult), "v")
                        for ei in range(8):
                            v.wait_ge(sems["p"], P0 + 12 + ei + 1)
                            b = 4 + ei % 2
                            xs = x_sb[:, ei * S + ic * 512:ei * S + ic * 512 + 512]
                            inc(v.tensor_tensor(out=xs, in0=xs, in1=pbk[b][:, :],
                                                op=ALU.add), "v")
                    assert tick["v"] == V0 + 19

                    @blk.sync
                    def _(sy):
                        for j in range(4):
                            nm, t = wsguard[0]
                            sy.wait_ge(sems[nm], t)
                            for dt_ in range(NT):
                                inc(sy.dma_start(
                                    out=wstage[:, dt_ * 256:(dt_ + 1) * 256],
                                    in_=ovw[dt_ * 128:(dt_ + 1) * 128,
                                            256 * j:256 * j + 256]), "q", 16)
                            wsguard[0] = ("w", W0 + j + 1)
                            globals()["_ov_qt_%d_%d" % (ic, j)] = tick["q"]

                    @blk.gpsimd
                    def _(gp):
                        for j in range(4):
                            gp.wait_ge(sems["q"],
                                       globals()["_ov_qt_%d_%d" % (ic, j)])
                            if j >= 2:
                                gp.wait_ge(sems["p"], ctx_mm_p[j - 2])
                            else:
                                hwait(gp, j % 2)
                            inc(gp.tensor_copy(
                                out=wbf[:, (j % 2) * 2048:(j % 2 + 1) * 2048],
                                in_=wstage[:, :]), "w")
                    assert tick["w"] == W0 + 4
                    halfg[0] = ("p", ctx_mm_p[2])
                    halfg[1] = ("p", ctx_mm_p[3])

        if CUT >= 8:
            emit_ffn(4)

        with nc.Block() as blk:
            fv = tick["v"]

            @blk.gpsimd
            def _(gp):
                gp.wait_ge(sems["v"], fv)
                for dt_ in range(NT):
                    inc(gp.dma_start(out=y[dt_ * 128:(dt_ + 1) * 128, :],
                                     in_=x_sb[:, dt_ * S:(dt_ + 1) * S]), "s", 16)
                gp.wait_ge(sems["s"], tick["s"])

    return nc


_NC_CACHE = []


def kernel(**inputs):
    B = 8
    x = np.asarray(inputs["L0_attn_a_x"])
    wqkvT = np.ascontiguousarray(np.asarray(inputs["L0_attn_a_w_qkv"]).T)
    woutT = np.ascontiguousarray(np.asarray(inputs["L0_attn_a_w_out"]).T)
    wfT = np.ascontiguousarray(np.asarray(inputs["L0_ffn_weight_fp8"]).T)
    qk_ = np.ascontiguousarray(np.asarray(inputs["L0_attn_b_qk"]))
    ov_ = np.ascontiguousarray(np.asarray(inputs["L0_attn_b_ov"]))
    lng = np.ascontiguousarray(np.asarray(inputs["L0_attn_a_ln_g"]).reshape(NT, 128).T)
    lnb = np.ascontiguousarray(np.asarray(inputs["L0_attn_a_ln_b"]).reshape(NT, 128).T)
    fb = np.ascontiguousarray(np.asarray(inputs["L0_ffn_bias"]).reshape(NT, 128).T)
    bo = np.ascontiguousarray(np.asarray(inputs["L0_attn_a_b_out"]).reshape(NT, 128).T)
    scl = np.concatenate([np.asarray(inputs["L0_ffn_input_scale"]).reshape(1),
                          np.asarray(inputs["L0_ffn_weight_scale"]).reshape(1)])
    scl = np.ascontiguousarray(scl.astype(np.float32))

    half = 32
    inv_freq = (np.float32(1.0) /
                np.float32(10000.0) ** (np.arange(half, dtype=np.float32) /
                                        np.float32(half))).astype(np.float32)
    freqs = (np.arange(S, dtype=np.float32)[:, None] *
             inv_freq[None, :]).astype(np.float32)
    cosT = np.cos(freqs).astype(np.float32)
    sinT = np.sin(freqs).astype(np.float32)
    C128 = np.tile(cosT.T, (4, 1)).astype(np.float32)
    S128 = np.tile(sinT.T, (4, 1)).astype(np.float32)
    S128[0:32] *= -1.0
    S128[64:96] *= -1.0
    C128 = np.ascontiguousarray(C128)
    S128 = np.ascontiguousarray(S128)

    jj = np.arange(128)[:, None]
    ii = np.arange(512)[None, :]
    maskd = np.concatenate(
        [(128 * r + jj <= ii).astype(np.float32) for r in range(4)],
        axis=1)
    maskd = np.ascontiguousarray(maskd)
    ident = np.ascontiguousarray(np.eye(128, dtype=np.float32))

    if not _NC_CACHE:
        _NC_CACHE.append(build())
    nc = _NC_CACHE[0]

    in_maps = []
    for b in range(B):
        in_maps.append({
            "xT": np.ascontiguousarray(x[b].T.astype(np.float32)),
            "wqkvT": wqkvT, "woutT": woutT, "wfT": wfT,
            "qkw": qk_, "ovw": ov_,
            "lng": lng, "lnb": lnb, "fbias": fb, "boutw": bo,
            "scl2": scl, "ropeC": C128, "ropeS": S128,
            "maskd": maskd, "identin": ident,
        })
    res = run_bass_kernel_spmd(nc, in_maps, core_ids=list(range(B)))
    out = np.stack([np.asarray(r["y"]).T for r in res.results], axis=0)
    return np.ascontiguousarray(out.astype(np.float32))


if __name__ == "__main__":
    nc = build()
    print("build ok")
